# version 05
# baseline (speedup 1.0000x reference)
import numpy as np
import ml_dtypes

import concourse.bass as bass
import concourse.mybir as mybir
from concourse.bass_utils import run_bass_kernel_spmd

F32 = mybir.dt.float32
BF16 = mybir.dt.bfloat16
AF = mybir.ActivationFunctionType
OP = mybir.AluOpType
AX = mybir.AxisListType

N_CORES = 8
B, S, D = 4, 4096, 4096
TOK = (B * S) // N_CORES
NT = TOK // 128
ND = D // 128
OC = 256
NO = D // OC
NG = NO * NT
NH = 2
HW = D // NH
GH = 136
EPS = 1e-5


def build_nc():
    nc = bass.Bass("TRN2", target_bir_lowering=False,
                   dynamic_dma_scratch_size=14336)

    x = nc.dram_tensor("x", [TOK, D], F32, kind="ExternalInput")
    wt = nc.dram_tensor("wt", [D, D], F32, kind="ExternalInput")
    wbt = nc.dram_tensor("wbt", [D, D], F32, kind="ExternalInput")
    wga = nc.dram_tensor("wga", [D, GH], F32, kind="ExternalInput")
    bfl = nc.dram_tensor("bfl", [128, D], F32, kind="ExternalInput")
    ba = nc.dram_tensor("ba", [D], F32, kind="ExternalInput")
    bb = nc.dram_tensor("bb", [D], F32, kind="ExternalInput")
    idn = nc.dram_tensor("idn", [128, 128], BF16, kind="ExternalInput")
    out = nc.dram_tensor("out", [TOK, D], F32, kind="ExternalOutput")
    wmg = nc.dram_tensor("wmg", [NO * 128, ND * OC], BF16)
    bmg = nc.dram_tensor("bmg", [D], F32)
    bmgb = nc.dram_tensor("bmgb", [D], BF16)
    bmgB = nc.dram_tensor("bmgB", [128, D], BF16)
    bflg = nc.dram_tensor("bflg", [128, D], BF16)

    import contextlib
    with contextlib.ExitStack() as st:
        def sb(name, shape, dt):
            return st.enter_context(nc.sbuf_tensor(name, shape, dt))

        def ps(name, shape, dt):
            return st.enter_context(nc.psum_tensor(name, shape, dt))

        def sem(name):
            return st.enter_context(nc.semaphore(name))

        guard = sb("guard", [128, 1024], BF16)
        xn_fm = sb("xn_fm", [128, NT * ND * 128], BF16)
        wm = [sb("wm0", [128, ND * OC], BF16),
              sb("wm1", [128, ND * OC], BF16)]
        xh_a = sb("xh_a", [128, 3 * HW], F32)
        xh = [xh_a[:, i * HW:(i + 1) * HW] for i in range(3)]
        xnb = sb("xnb", [128, HW // 2], BF16)
        wga_sb = sb("wga_sb", [128, ND * GH], BF16)
        hwfm = sb("hwfm", [128, TOK], BF16)
        bfl_sl = sb("bfl_sl", [128, 2 * OC], BF16)
        outsb_a = sb("outsb_a", [128, 2 * OC], F32)
        outsb = [outsb_a[:, i * OC:(i + 1) * OC] for i in range(2)]
        xres_a = sb("xres_a", [128, 3 * OC], F32)
        xres = [xres_a[:, i * OC:(i + 1) * OC] for i in range(3)]
        bmsl = sb("bmsl", [128, 2 * OC], BF16)
        ident = sb("ident", [128, 128], BF16)
        ar = sb("arena", [128, 128], F32)
        stats = ar[:, 0:48]
        mv = ar[:, 48:50]
        varp = ar[:, 50:51]
        lnv = ar[:, 51:52]
        inv = ar[:, 52:53]
        ls = ar[:, 56:64]
        e_ = ar[:, 64:72]
        tmask = ar[:, 72:80]
        ls2 = ar[:, 80:88]
        keep = ar[:, 88:96]
        ew = ar[:, 96:104]
        m1 = ar[:, 104:105]
        m2 = ar[:, 105:106]
        ssum = ar[:, 106:107]
        rinv = ar[:, 107:108]
        dumf = ar[:, 108:112]
        hwb = sb("hwb", [128, 128], BF16)

        pbig = [ps(f"pbig{i}", [128, OC], F32) for i in range(4)]
        pgh = [ps("pgh0", [128, GH], F32), ps("pgh1", [128, GH], F32)]
        ptr0w = ps("ptr0", [128, 768], BF16)
        ptr = [ptr0w[:, 0:512], ps("ptr1", [128, 512], BF16)]
        phw = ptr0w[:, 512:640]

        s_const = sem("s_const")
        s_ident = sem("s_ident")
        s_xh3 = [sem(f"s_xh{i}") for i in range(3)]
        s_fl3 = [sem(f"s_fl{i}") for i in range(3)]
        s_xn = sem("s_xn")
        s_varp = sem("s_varp")
        s_inv = sem("s_inv")
        s_tr = sem("s_tr")
        s_trfree = sem("s_trfree")
        s_gh = sem("s_gh")
        s_ls = sem("s_ls")
        s_e = sem("s_e")
        s_ghfree = sem("s_ghfree")
        s_hwtr = sem("s_hwtr")
        s_hwfm = sem("s_hwfm")
        s_wm_p = [sem("s_wm0"), sem("s_wm1")]
        s_wml = [sem("s_wml0"), sem("s_wml1")]
        s_wmfree = sem("s_wmfree")
        s_bm_p = [sem("s_bm0"), sem("s_bm1")]
        s_xr = [sem(f"s_xr{i}") for i in range(3)]
        s_mm = sem("s_mm")
        s_evac = sem("s_evac")
        s_st_p = [sem("s_st0"), sem("s_st1")]
        s_wfin = sem("s_wfin")
        s_bmg = sem("s_bmg")

        wt_r = wt[:, :].rearrange("(c p) o -> p c o", p=128)
        wbt_r = wbt[:, :].rearrange("(c p) o -> p c o", p=128)
        wga_r = wga[:, :].rearrange("(c p) g -> p c g", p=128)
        wga_dst = wga_sb[:, :].rearrange("p (c g) -> p c g", g=GH)

        def wm_dst(buf):
            return wm[buf][:, :].rearrange("p (c o) -> p c o", o=OC)

        def fm(t, d):
            off = (t * ND + d) * 128
            return xn_fm[:, off:off + 128]

        NTG = NT

        def kt(gi):
            return gi // NTG, gi % NTG

        with nc.Block() as block:

            @block.sync
            def _(sync):
                sync.dma_start(out=ident[:, :], in_=idn[:, :]).then_inc(s_ident, 16)
                for t in range(NT):
                    for h in range(NH):
                        li = t * NH + h
                        b3 = li % 3
                        if li >= 3:
                            sync.wait_ge(s_xn, 2 * li - 4)
                        sync.dma_start(
                            out=xh[b3],
                            in_=x[t * 128:(t + 1) * 128, h * HW:(h + 1) * HW],
                        ).then_inc(s_xh3[b3], 16)
                        sync.dma_start(
                            out=dumf,
                            in_=x[t * 128:(t + 1) * 128, 0:4],
                        ).then_inc(s_fl3[b3], 16)

                def sync_fill_bm(k):
                    sync.dma_start(
                        out=bmsl[:, (k % 2) * OC:(k % 2 + 1) * OC],
                        in_=bmgB[:, k * OC:(k + 1) * OC],
                    ).then_inc(s_bm_p[k % 2], 16)
                    sync.dma_start(
                        out=bfl_sl[:, (k % 2) * OC:(k % 2 + 1) * OC],
                        in_=bflg[:, k * OC:(k + 1) * OC],
                    ).then_inc(s_bm_p[k % 2], 16)

                sync.wait_ge(s_wm_p[0], 32)
                sync.wait_ge(s_wm_p[1], 32)
                sync.wait_ge(s_bmg, 80)
                sync.dma_start(out=wm[0][:, :],
                               in_=wmg[0:128, :]).then_inc(s_wml[0], 16)
                sync.dma_start(out=wm[1][:, :],
                               in_=wmg[128:256, :]).then_inc(s_wml[1], 16)
                sync_fill_bm(0)
                sync_fill_bm(1)
                for gi in range(3):
                    k, t = kt(gi)
                    sync.dma_start(
                        out=xres[gi % 3],
                        in_=x[t * 128:(t + 1) * 128, k * OC:(k + 1) * OC],
                    ).then_inc(s_xr[gi % 3], 16)
                for gi in range(NG):
                    k, t = kt(gi)
                    sync.wait_ge(s_evac, gi + 1)
                    sync.dma_start(
                        out=out[t * 128:(t + 1) * 128, k * OC:(k + 1) * OC],
                        in_=outsb[gi % 2],
                    ).then_inc(s_st_p[gi % 2], 16)
                    if gi % NTG == NTG - 1 and k + 2 < NO:
                        kk = k + 2
                        sync.wait_ge(s_wmfree, k + 1)
                        sync.wait_ge(s_wm_p[kk % 2], 32 * (kk // 2 + 1))
                        sync.dma_start(
                            out=wm[kk % 2][:, :],
                            in_=wmg[kk * 128:(kk + 1) * 128, :],
                        ).then_inc(s_wml[kk % 2], 16)
                        sync_fill_bm(kk)
                    if gi + 3 < NG:
                        k2, t2 = kt(gi + 3)
                        sync.dma_start(
                            out=xres[gi % 3],
                            in_=x[t2 * 128:(t2 + 1) * 128,
                                  k2 * OC:(k2 + 1) * OC],
                        ).then_inc(s_xr[gi % 3], 16)
                sync.wait_ge(s_st_p[0], 16 * NG // 2)
                sync.wait_ge(s_st_p[1], 16 * NG // 2)


            @block.gpsimd
            def _(g):
                g.dma_start(out=bflg[:, :], in_=bfl[:, :],
                            max_dma_last_dim=8192).then_inc(s_bmg, 16)
                g.dma_start(out=wga_dst, in_=wga_r).then_inc(s_const, 16)
                g.dma_start(out=bmg[:], in_=ba[:]).then_inc(s_bmg, 16)
                g.wait_ge(s_bmg, 32)
                g.dma_start(out=bmg[:], in_=bb[:],
                            accum_op=OP.add).then_inc(s_bmg, 16)
                g.wait_ge(s_bmg, 48)
                g.dma_start(out=bmgb[:], in_=bmg[:],
                            max_dma_last_dim=8192).then_inc(s_bmg, 16)
                g.wait_ge(s_bmg, 64)
                g.dma_start(out=bmgB[:, :],
                            in_=bass.AP(bmgb, 0, [[0, 128], [1, D]]),
                            max_dma_last_dim=4096).then_inc(s_bmg, 16)

                def fill_wm(k):
                    dst = wmg[k * 128:(k + 1) * 128, :].rearrange(
                        "p (c o) -> p c o", o=OC)
                    g.dma_start(out=dst,
                                in_=wt_r[:, :, k * OC:(k + 1) * OC]
                                ).then_inc(s_wm_p[k % 2], 16)
                    g.dma_start(out=dst,
                                in_=wbt_r[:, :, k * OC:(k + 1) * OC],
                                accum_op=OP.add).then_inc(s_wm_p[k % 2], 16)

                def fill_bm(k):
                    pass

                fill_wm(0)
                fill_bm(0)
                g.wait_ge(s_wm_p[0], 32)
                fill_wm(1)
                fill_bm(1)
                for k in range(NO - 2):
                    g.wait_ge(s_wmfree, k + 1)
                    fill_wm(k + 2)
                    fill_bm(k + 2)


            @block.scalar
            def _(scalar):
                for t in range(NT):
                    scalar.wait_ge(s_varp, t + 1)
                    scalar.copy(out=ar[:, 121:122], in_=ar[:, 122:123])
                    scalar.copy(out=ar[:, 121:122], in_=ar[:, 122:123])
                    scalar.activation(out=varp, in_=mv[:, 1:2],
                                      func=AF.Copy, bias=EPS)
                    scalar.activation(out=lnv, in_=varp, func=AF.Ln)
                    scalar.activation(out=inv, in_=lnv, func=AF.Exp,
                                      scale=-0.5)
                    scalar.drain()
                    scalar.sem_inc(s_inv, 1)
                    for b in range(8):
                        scalar.wait_ge(s_tr, t * 8 + b + 1)
                        off = (t * ND + b * 4) * 128
                        ins = scalar.copy(
                            out=xn_fm[:, off:off + 512],
                            in_=ptr[b % 2][:, :],
                        )
                        if b == 7:
                            scalar.drain()
                            scalar.sem_inc(s_trfree, 1)
                        else:
                            ins.then_inc(s_trfree, 1)
                    if t >= 1:
                        scalar.wait_ge(s_ls, t)
                        scalar.copy(out=ar[:, 121:122], in_=ar[:, 122:123])
                        scalar.copy(out=ar[:, 121:122], in_=ar[:, 122:123])
                        scalar.activation(out=e_, in_=ls, func=AF.Exp)
                        scalar.drain()
                        scalar.sem_inc(s_e, 1)
                        scalar.wait_ge(s_hwtr, t)
                        scalar.copy(out=ar[:, 121:122], in_=ar[:, 122:123])
                        scalar.copy(out=ar[:, 121:122], in_=ar[:, 122:123])
                        scalar.copy(
                            out=hwfm[:, (t - 1) * 128:t * 128],
                            in_=phw[:, :],
                        )
                        scalar.drain()
                        scalar.sem_inc(s_hwfm, 1)
                if True:
                    scalar.wait_ge(s_ls, NT)
                    scalar.copy(out=ar[:, 121:122], in_=ar[:, 122:123])
                    scalar.copy(out=ar[:, 121:122], in_=ar[:, 122:123])
                    scalar.activation(out=e_, in_=ls, func=AF.Exp)
                    scalar.drain()
                    scalar.sem_inc(s_e, 1)
                    scalar.wait_ge(s_hwtr, NT)
                    scalar.copy(out=ar[:, 121:122], in_=ar[:, 122:123])
                    scalar.copy(out=ar[:, 121:122], in_=ar[:, 122:123])
                    scalar.copy(
                        out=hwfm[:, (NT - 1) * 128:NT * 128],
                        in_=phw[:, :],
                    )
                    scalar.drain()
                    scalar.sem_inc(s_hwfm, 1)

            @block.vector
            def _(v):
                for t in range(NT):
                    if t == 0:
                        v.wait_ge(s_wm_p[0], 32)
                        v.wait_ge(s_wm_p[1], 32)
                    for h in range(NH):
                        li = t * NH + h
                        b3 = li % 3
                        v.wait_ge(s_xh3[b3], 16 * (li // 3 + 1))
                        v.wait_ge(s_fl3[b3], 16 * (li // 3 + 1))
                        for j in range(4):
                            v.bn_stats(
                                out=stats[:, (h * 4 + j) * 6:(h * 4 + j + 1) * 6],
                                in_=xh[b3][:, j * 512:(j + 1) * 512])
                    v.bn_aggr(out=mv, in_=stats)
                    v.drain()
                    v.sem_inc(s_varp, 1)
                    v.wait_ge(s_inv, t + 1)
                    v.tensor_copy(out=ar[:, 124:126], in_=ar[:, 126:128])
                    v.tensor_copy(out=ar[:, 124:126], in_=ar[:, 126:128])
                    for q in range(4):
                        li = t * NH + q // 2
                        w = 8 * t + 2 * q
                        if w > 0:
                            v.wait_ge(s_tr, w)
                        v.tensor_scalar(
                            out=xnb[:, :],
                            in0=xh[li % 3][:, (q % 2) * 1024:(q % 2 + 1) * 1024],
                            scalar1=mv[:, 0:1], scalar2=inv,
                            op0=OP.subtract, op1=OP.mult,
                        )
                        v.drain()
                        v.sem_inc(s_xn, 1)
                    if t == 0:
                        continue
                    t = t - 1
                    v.wait_ge(s_gh, min(t + 2, NT + 1))
                    if t >= 1:
                        v.wait_ge(s_hwtr, t)
                    v.reduce_max(out=m1, in_=pgh[t % 2][:, 0:8], axis=AX.X)
                    v.tensor_copy(out=ar[:, 124:126], in_=ar[:, 126:128])
                    v.tensor_copy(out=ar[:, 124:126], in_=ar[:, 126:128])
                    v.tensor_scalar(out=ls, in0=pgh[t % 2][:, 0:8],
                                    scalar1=m1, scalar2=0.0,
                                    op0=OP.subtract, op1=OP.min)
                    v.tensor_copy(out=ar[:, 124:126], in_=ar[:, 126:128])
                    v.tensor_copy(out=ar[:, 124:126], in_=ar[:, 126:128])
                    v.drain()
                    v.sem_inc(s_ls, 1)
                    v.tensor_scalar(out=tmask, in0=ls, scalar1=0.0,
                                    scalar2=None, op0=OP.is_ge)
                    v.tensor_copy(out=ar[:, 124:126], in_=ar[:, 126:128])
                    v.tensor_copy(out=ar[:, 124:126], in_=ar[:, 126:128])
                    v.scalar_tensor_tensor(out=ls2, in0=tmask,
                                           scalar=-1e30, in1=ls,
                                           op0=OP.mult, op1=OP.add)
                    v.tensor_copy(out=ar[:, 124:126], in_=ar[:, 126:128])
                    v.tensor_copy(out=ar[:, 124:126], in_=ar[:, 126:128])
                    v.reduce_max(out=m2, in_=ls2, axis=AX.X)
                    v.tensor_copy(out=ar[:, 124:126], in_=ar[:, 126:128])
                    v.tensor_copy(out=ar[:, 124:126], in_=ar[:, 126:128])
                    v.tensor_scalar(out=keep, in0=ls, scalar1=m2,
                                    scalar2=None, op0=OP.is_ge)
                    v.tensor_copy(out=ar[:, 124:126], in_=ar[:, 126:128])
                    v.tensor_copy(out=ar[:, 124:126], in_=ar[:, 126:128])
                    v.wait_ge(s_e, t + 1)
                    v.tensor_copy(out=ar[:, 124:126], in_=ar[:, 126:128])
                    v.tensor_copy(out=ar[:, 124:126], in_=ar[:, 126:128])
                    v.tensor_tensor(out=ew, in0=e_, in1=keep, op=OP.mult)
                    v.tensor_copy(out=ar[:, 124:126], in_=ar[:, 126:128])
                    v.tensor_copy(out=ar[:, 124:126], in_=ar[:, 126:128])
                    v.tensor_reduce(out=ssum, in_=ew, axis=AX.X, op=OP.add)
                    v.tensor_copy(out=ar[:, 124:126], in_=ar[:, 126:128])
                    v.tensor_copy(out=ar[:, 124:126], in_=ar[:, 126:128])
                    v.tensor_scalar(out=ar[:, 30:31], in0=ssum, scalar1=1e-20,
                                    scalar2=None, op0=OP.max)
                    v.tensor_copy(out=ar[:, 124:126], in_=ar[:, 126:128])
                    v.tensor_copy(out=ar[:, 124:126], in_=ar[:, 126:128])
                    v.reciprocal(out=rinv, in_=ar[:, 30:31])
                    v.tensor_copy(out=ar[:, 124:126], in_=ar[:, 126:128])
                    v.tensor_copy(out=ar[:, 124:126], in_=ar[:, 126:128])
                    for e in range(8):
                        v.tensor_scalar(
                            out=hwb[:, e * 16:(e + 1) * 16],
                            in0=pgh[t % 2][:, 8 + e * 16:8 + (e + 1) * 16],
                            scalar1=ew[:, e:e + 1], scalar2=rinv,
                            op0=OP.mult, op1=OP.mult)
                    v.drain()
                    v.sem_inc(s_ghfree, 1)
                if True:
                    t = NT - 1
                    v.wait_ge(s_gh, NT + 1)
                    v.wait_ge(s_hwtr, t)
                    v.reduce_max(out=m1, in_=pgh[t % 2][:, 0:8], axis=AX.X)
                    v.tensor_copy(out=ar[:, 124:126], in_=ar[:, 126:128])
                    v.tensor_copy(out=ar[:, 124:126], in_=ar[:, 126:128])
                    v.tensor_scalar(out=ls, in0=pgh[t % 2][:, 0:8],
                                    scalar1=m1, scalar2=0.0,
                                    op0=OP.subtract, op1=OP.min)
                    v.tensor_copy(out=ar[:, 124:126], in_=ar[:, 126:128])
                    v.tensor_copy(out=ar[:, 124:126], in_=ar[:, 126:128])
                    v.drain()
                    v.sem_inc(s_ls, 1)
                    v.tensor_scalar(out=tmask, in0=ls, scalar1=0.0,
                                    scalar2=None, op0=OP.is_ge)
                    v.tensor_copy(out=ar[:, 124:126], in_=ar[:, 126:128])
                    v.tensor_copy(out=ar[:, 124:126], in_=ar[:, 126:128])
                    v.scalar_tensor_tensor(out=ls2, in0=tmask,
                                           scalar=-1e30, in1=ls,
                                           op0=OP.mult, op1=OP.add)
                    v.tensor_copy(out=ar[:, 124:126], in_=ar[:, 126:128])
                    v.tensor_copy(out=ar[:, 124:126], in_=ar[:, 126:128])
                    v.reduce_max(out=m2, in_=ls2, axis=AX.X)
                    v.tensor_copy(out=ar[:, 124:126], in_=ar[:, 126:128])
                    v.tensor_copy(out=ar[:, 124:126], in_=ar[:, 126:128])
                    v.tensor_scalar(out=keep, in0=ls, scalar1=m2,
                                    scalar2=None, op0=OP.is_ge)
                    v.tensor_copy(out=ar[:, 124:126], in_=ar[:, 126:128])
                    v.tensor_copy(out=ar[:, 124:126], in_=ar[:, 126:128])
                    v.wait_ge(s_e, t + 1)
                    v.tensor_copy(out=ar[:, 124:126], in_=ar[:, 126:128])
                    v.tensor_copy(out=ar[:, 124:126], in_=ar[:, 126:128])
                    v.tensor_tensor(out=ew, in0=e_, in1=keep, op=OP.mult)
                    v.tensor_copy(out=ar[:, 124:126], in_=ar[:, 126:128])
                    v.tensor_copy(out=ar[:, 124:126], in_=ar[:, 126:128])
                    v.tensor_reduce(out=ssum, in_=ew, axis=AX.X, op=OP.add)
                    v.tensor_copy(out=ar[:, 124:126], in_=ar[:, 126:128])
                    v.tensor_copy(out=ar[:, 124:126], in_=ar[:, 126:128])
                    v.tensor_scalar(out=ar[:, 30:31], in0=ssum, scalar1=1e-20,
                                    scalar2=None, op0=OP.max)
                    v.tensor_copy(out=ar[:, 124:126], in_=ar[:, 126:128])
                    v.tensor_copy(out=ar[:, 124:126], in_=ar[:, 126:128])
                    v.reciprocal(out=rinv, in_=ar[:, 30:31])
                    v.tensor_copy(out=ar[:, 124:126], in_=ar[:, 126:128])
                    v.tensor_copy(out=ar[:, 124:126], in_=ar[:, 126:128])
                    for e in range(8):
                        v.tensor_scalar(
                            out=hwb[:, e * 16:(e + 1) * 16],
                            in0=pgh[t % 2][:, 8 + e * 16:8 + (e + 1) * 16],
                            scalar1=ew[:, e:e + 1], scalar2=rinv,
                            op0=OP.mult, op1=OP.mult)
                    v.drain()
                    v.sem_inc(s_ghfree, 1)

                for gi in range(NG):
                    k, t = kt(gi)
                    v.wait_ge(s_mm, min(gi + 3, NG + 1))
                    if gi % NTG == 0:
                        v.wait_ge(s_bm_p[k % 2], 32 * (k // 2 + 1))
                    v.wait_ge(s_xr[gi % 3], 16 * (gi // 3 + 1))
                    if gi >= 2:
                        v.wait_ge(s_st_p[gi % 2], 16 * (gi // 2))
                    v.tensor_tensor(
                        out=outsb[gi % 2],
                        in0=pbig[gi % 4][:, :],
                        in1=bmsl[:, (k % 2) * OC:(k % 2 + 1) * OC],
                        op=OP.add,
                    )
                    v.tensor_tensor(
                        out=outsb[gi % 2],
                        in0=outsb[gi % 2],
                        in1=xres[gi % 3],
                        op=OP.add,
                    )
                    v.drain()
                    v.sem_inc(s_evac, 1)

            @block.tensor
            def _(te):
                te.wait_ge(s_ident, 16)
                te.wait_ge(s_fl3[0], 16)
                te.wait_ge(s_fl3[1], 16)
                for t in range(NT):
                    for b in range(8):
                        gb = t * 8 + b
                        te.wait_ge(s_xn, t * 4 + b // 2 + 1)
                        if gb >= 2:
                            te.wait_ge(s_trfree, gb - 1)
                        for j in range(4):
                            col = (b % 2) * 512 + j * 128
                            ins = te.transpose(
                                out=ptr[b % 2][:, j * 128:(j + 1) * 128],
                                in_=xnb[:, col:col + 128],
                                identity=ident[:, :])
                        ins.then_inc(s_tr, 1)
                    if t == 0:
                        te.wait_ge(s_const, 16)
                        te.wait_ge(s_wm_p[0], 32)
                    te.wait_ge(s_trfree, 8 * (t + 1))
                    if t > 1:
                        te.wait_ge(s_ghfree, t - 1)
                    for d in range(ND):
                        ins = te.matmul(
                            out=pgh[t % 2][:, :],
                            lhsT=fm(t, d),
                            rhs=wga_sb[:, d * GH:(d + 1) * GH],
                            start=(d == 0), stop=(d == ND - 1))
                    ins.then_inc(s_gh, 1)
                    if t > 0:
                        te.wait_ge(s_ghfree, t)
                        if t > 1:
                            te.wait_ge(s_hwfm, t - 1)
                        te.transpose(out=phw[:, :], in_=hwb[:, :],
                                     identity=ident[:, :])
                        te.matmul(out=pbig[0][:, 0:128], lhsT=ident[:, :],
                                  rhs=ident[:, :], start=True,
                                  stop=True).then_inc(s_hwtr, 1)
                te.matmul(out=pbig[0][:, 0:128], lhsT=ident[:, :],
                          rhs=ident[:, :], start=True,
                          stop=True).then_inc(s_gh, 1)
                te.wait_ge(s_ghfree, NT)
                te.wait_ge(s_hwfm, NT - 1)
                te.transpose(out=phw[:, :], in_=hwb[:, :],
                             identity=ident[:, :])
                te.matmul(out=pbig[0][:, 0:128], lhsT=ident[:, :],
                          rhs=ident[:, :], start=True,
                          stop=True).then_inc(s_hwtr, 1)

                for gi in range(NG):
                    k, t = kt(gi)
                    if gi % NTG == 0:
                        te.wait_ge(s_wml[k % 2], 16 * (k // 2 + 1))
                        te.wait_ge(s_bm_p[k % 2], 32 * (k // 2 + 1))

                    if gi >= 4:
                        te.wait_ge(s_evac, gi - 3)
                    for d in range(ND):
                        te.matmul(
                            out=pbig[gi % 4][:, :],
                            lhsT=fm(t, d),
                            rhs=wm[k % 2][:, d * OC:(d + 1) * OC],
                            start=(d == 0), stop=False)
                    if k == 0:
                        te.wait_ge(s_hwfm, t + 1)
                    te.matmul(
                        out=pbig[gi % 4][:, :],
                        lhsT=hwfm[:, t * 128:(t + 1) * 128],
                        rhs=bfl_sl[:, (k % 2) * OC:(k % 2 + 1) * OC],
                        start=False, stop=True).then_inc(s_mm, 1)
                    if gi % NTG == NTG - 1:
                        te.matmul(out=pgh[0][:, 0:128], lhsT=ident[:, :],
                                  rhs=ident[:, :], start=True,
                                  stop=True).then_inc(s_wmfree, 1)
                te.transpose(out=ptr[0][:, 0:128], in_=ident[:, :],
                             identity=ident[:, :])
                te.transpose(out=ptr[0][:, 128:256], in_=ident[:, :],
                             identity=ident[:, :]).then_inc(s_mm, 1)

    return nc


def kernel(L0_attn_x, L0_attn_weight, L0_attn_bias, L0_ffn_W_base,
           L0_ffn_b_base, L0_ffn_W_gate, L0_ffn_A, L0_ffn_B):
    x_full = np.ascontiguousarray(
        np.asarray(L0_attn_x, dtype=np.float32).reshape(B * S, D))
    wt = np.ascontiguousarray(np.asarray(L0_attn_weight, np.float32).T)
    wbt = np.ascontiguousarray(np.asarray(L0_ffn_W_base, np.float32).T)
    wg = np.asarray(L0_ffn_W_gate, np.float32)
    a = np.asarray(L0_ffn_A, np.float32).reshape(128, D)
    wga = np.ascontiguousarray(
        np.concatenate([wg.T, a.T], axis=1))
    bfl = np.ascontiguousarray(
        np.asarray(L0_ffn_B, np.float32).transpose(0, 2, 1).reshape(128, D))
    ba = np.ascontiguousarray(np.asarray(L0_attn_bias, np.float32))
    bb = np.ascontiguousarray(np.asarray(L0_ffn_b_base, np.float32))
    idn = np.eye(128, dtype=ml_dtypes.bfloat16)

    nc = build_nc()
    in_maps = []
    for c in range(N_CORES):
        in_maps.append({
            "x": np.ascontiguousarray(x_full[c * TOK:(c + 1) * TOK]),
            "wt": wt, "wbt": wbt, "wga": wga, "bfl": bfl,
            "ba": ba, "bb": bb, "idn": idn,
        })
    res = run_bass_kernel_spmd(nc, in_maps, core_ids=list(range(N_CORES)))
    shards = [np.asarray(r["out"], dtype=np.float32) for r in res.results]
    return np.concatenate(shards, axis=0).reshape(B, S, D)


# version 06
# speedup vs baseline: 1.1769x; 1.1769x over previous
import numpy as np
import ml_dtypes

import concourse.bass as bass
import concourse.mybir as mybir
from concourse.bass_utils import run_bass_kernel_spmd

F32 = mybir.dt.float32
BF16 = mybir.dt.bfloat16
AF = mybir.ActivationFunctionType
OP = mybir.AluOpType
AX = mybir.AxisListType

N_CORES = 8
B, S, D = 4, 4096, 4096
TOK = (B * S) // N_CORES
NT = TOK // 128
ND = D // 128
OC = 256
NO = D // OC
NG = NO * NT
NH = 2
HW = D // NH
GH = 136
EPS = 1e-5


def build_nc():
    nc = bass.Bass("TRN2", target_bir_lowering=False,
                   dynamic_dma_scratch_size=14336)

    x = nc.dram_tensor("x", [TOK, D], F32, kind="ExternalInput")
    wtp = nc.dram_tensor("wtp", [NO * 128, ND * OC], F32, kind="ExternalInput")
    wbtp = nc.dram_tensor("wbtp", [NO * 128, ND * OC], F32, kind="ExternalInput")
    wga = nc.dram_tensor("wga", [D, GH], F32, kind="ExternalInput")
    bfl = nc.dram_tensor("bfl", [128, D], F32, kind="ExternalInput")
    ba = nc.dram_tensor("ba", [D], F32, kind="ExternalInput")
    bb = nc.dram_tensor("bb", [D], F32, kind="ExternalInput")
    idn = nc.dram_tensor("idn", [128, 128], BF16, kind="ExternalInput")
    out = nc.dram_tensor("out", [TOK, D], F32, kind="ExternalOutput")
    bmg = nc.dram_tensor("bmg", [D], F32)
    bmgb = nc.dram_tensor("bmgb", [D], BF16)
    bmgB = nc.dram_tensor("bmgB", [128, D], BF16)
    bflg = nc.dram_tensor("bflg", [128, D], BF16)

    import contextlib
    with contextlib.ExitStack() as st:
        def sb(name, shape, dt):
            return st.enter_context(nc.sbuf_tensor(name, shape, dt))

        def ps(name, shape, dt):
            return st.enter_context(nc.psum_tensor(name, shape, dt))

        def sem(name):
            return st.enter_context(nc.semaphore(name))

        guard = sb("guard", [128, 1024], BF16)
        xn_fm = sb("xn_fm", [128, NT * ND * 128], BF16)
        wm = [sb("wm0", [128, ND * OC], BF16),
              sb("wm1", [128, ND * OC], BF16)]
        xh_a = sb("xh_a", [128, 3 * HW], F32)
        xh = [xh_a[:, i * HW:(i + 1) * HW] for i in range(3)]
        xnb = sb("xnb", [128, HW // 2], BF16)
        wga_sb = sb("wga_sb", [128, ND * GH], BF16)
        hwfm = sb("hwfm", [128, TOK], BF16)
        bfl_sl = sb("bfl_sl", [128, 2 * OC], BF16)
        outsb_a = sb("outsb_a", [128, 2 * OC], F32)
        outsb = [outsb_a[:, i * OC:(i + 1) * OC] for i in range(2)]
        xres_a = sb("xres_a", [128, 3 * OC], F32)
        xres = [xres_a[:, i * OC:(i + 1) * OC] for i in range(3)]
        bmsl = sb("bmsl", [128, 2 * OC], BF16)
        ident = sb("ident", [128, 128], BF16)
        ar = sb("arena", [128, 128], F32)
        stats = ar[:, 0:48]
        mv = ar[:, 48:50]
        varp = ar[:, 50:51]
        lnv = ar[:, 51:52]
        inv = ar[:, 52:53]
        ls = ar[:, 56:64]
        e_ = ar[:, 64:72]
        tmask = ar[:, 72:80]
        ls2 = ar[:, 80:88]
        keep = ar[:, 88:96]
        ew = ar[:, 96:104]
        m1 = ar[:, 104:105]
        m2 = ar[:, 105:106]
        ssum = ar[:, 106:107]
        rinv = ar[:, 107:108]
        dumf = ar[:, 108:112]
        hwb = sb("hwb", [128, 128], BF16)

        pbig = [ps(f"pbig{i}", [128, OC], F32) for i in range(4)]
        pgh = [ps("pgh0", [128, GH], F32), ps("pgh1", [128, GH], F32)]
        ptr0w = ps("ptr0", [128, 768], BF16)
        ptr = [ptr0w[:, 0:512], ps("ptr1", [128, 512], BF16)]
        phw = ptr0w[:, 512:640]

        s_const = sem("s_const")
        s_ident = sem("s_ident")
        s_xh3 = [sem(f"s_xh{i}") for i in range(3)]
        s_fl3 = [sem(f"s_fl{i}") for i in range(3)]
        s_xn = sem("s_xn")
        s_varp = sem("s_varp")
        s_inv = sem("s_inv")
        s_tr = sem("s_tr")
        s_trfree = sem("s_trfree")
        s_gh = sem("s_gh")
        s_ls = sem("s_ls")
        s_e = sem("s_e")
        s_ghfree = sem("s_ghfree")
        s_hwtr = sem("s_hwtr")
        s_hwfm = sem("s_hwfm")
        s_wm_p = [sem("s_wm0"), sem("s_wm1")]
        s_wmfree = sem("s_wmfree")
        s_bm_p = [sem("s_bm0"), sem("s_bm1")]
        s_xr = [sem(f"s_xr{i}") for i in range(3)]
        s_mm = sem("s_mm")
        s_evac = sem("s_evac")
        s_st_p = [sem("s_st0"), sem("s_st1")]
        s_wfin = sem("s_wfin")
        s_bmg = sem("s_bmg")

        wga_r = wga[:, :].rearrange("(c p) g -> p c g", p=128)
        wga_dst = wga_sb[:, :].rearrange("p (c g) -> p c g", g=GH)

        def fm(t, d):
            off = (t * ND + d) * 128
            return xn_fm[:, off:off + 128]

        NTG = NT

        def kt(gi):
            return gi // NTG, gi % NTG

        with nc.Block() as block:

            @block.sync
            def _(sync):
                sync.dma_start(out=ident[:, :], in_=idn[:, :]).then_inc(s_ident, 16)
                for t in range(NT):
                    for h in range(NH):
                        li = t * NH + h
                        b3 = li % 3
                        if li >= 3:
                            sync.wait_ge(s_xn, 2 * li - 4)
                        sync.dma_start(
                            out=xh[b3],
                            in_=x[t * 128:(t + 1) * 128, h * HW:(h + 1) * HW],
                        ).then_inc(s_xh3[b3], 16)
                        sync.dma_start(
                            out=dumf,
                            in_=x[t * 128:(t + 1) * 128, 0:4],
                        ).then_inc(s_fl3[b3], 16)

                def sync_fill_bm(k):
                    sync.dma_start(
                        out=bmsl[:, (k % 2) * OC:(k % 2 + 1) * OC],
                        in_=bmgB[:, k * OC:(k + 1) * OC],
                    ).then_inc(s_bm_p[k % 2], 16)
                    sync.dma_start(
                        out=bfl_sl[:, (k % 2) * OC:(k % 2 + 1) * OC],
                        in_=bflg[:, k * OC:(k + 1) * OC],
                    ).then_inc(s_bm_p[k % 2], 16)

                sync.wait_ge(s_bmg, 80)
                sync_fill_bm(0)
                sync_fill_bm(1)
                for gi in range(3):
                    k, t = kt(gi)
                    sync.dma_start(
                        out=xres[gi % 3],
                        in_=x[t * 128:(t + 1) * 128, k * OC:(k + 1) * OC],
                    ).then_inc(s_xr[gi % 3], 16)
                for gi in range(NG):
                    k, t = kt(gi)
                    sync.wait_ge(s_evac, gi + 1)
                    sync.dma_start(
                        out=out[t * 128:(t + 1) * 128, k * OC:(k + 1) * OC],
                        in_=outsb[gi % 2],
                    ).then_inc(s_st_p[gi % 2], 16)
                    if gi % NTG == NTG - 1 and k + 2 < NO:
                        sync_fill_bm(k + 2)
                    if gi + 3 < NG:
                        k2, t2 = kt(gi + 3)
                        sync.dma_start(
                            out=xres[gi % 3],
                            in_=x[t2 * 128:(t2 + 1) * 128,
                                  k2 * OC:(k2 + 1) * OC],
                        ).then_inc(s_xr[gi % 3], 16)
                sync.wait_ge(s_st_p[0], 16 * NG // 2)
                sync.wait_ge(s_st_p[1], 16 * NG // 2)


            @block.gpsimd
            def _(g):
                g.dma_start(out=bflg[:, :], in_=bfl[:, :],
                            max_dma_last_dim=8192).then_inc(s_bmg, 16)
                g.dma_start(out=wga_dst, in_=wga_r).then_inc(s_const, 16)
                g.dma_start(out=bmg[:], in_=ba[:]).then_inc(s_bmg, 16)
                g.wait_ge(s_bmg, 32)
                g.dma_start(out=bmg[:], in_=bb[:],
                            accum_op=OP.add).then_inc(s_bmg, 16)
                g.wait_ge(s_bmg, 48)
                g.dma_start(out=bmgb[:], in_=bmg[:],
                            max_dma_last_dim=8192).then_inc(s_bmg, 16)
                g.wait_ge(s_bmg, 64)
                g.dma_start(out=bmgB[:, :],
                            in_=bass.AP(bmgb, 0, [[0, 128], [1, D]]),
                            max_dma_last_dim=4096).then_inc(s_bmg, 16)

                def fill_wm(k):
                    par = k % 2
                    base = 32 * (k // 2)
                    g.dma_start(out=wm[par][:, :],
                                in_=wtp[k * 128:(k + 1) * 128, :],
                                max_dma_last_dim=8192
                                ).then_inc(s_wm_p[par], 16)
                    g.wait_ge(s_wm_p[par], base + 16)
                    g.dma_start(out=wm[par][:, :],
                                in_=wbtp[k * 128:(k + 1) * 128, :],
                                accum_op=OP.add,
                                max_dma_last_dim=8192
                                ).then_inc(s_wm_p[par], 16)
                    g.wait_ge(s_wm_p[par], base + 32)

                fill_wm(0)
                fill_wm(1)
                for k in range(NO - 2):
                    g.wait_ge(s_wmfree, k + 1)
                    fill_wm(k + 2)


            @block.scalar
            def _(scalar):
                for t in range(NT):
                    scalar.wait_ge(s_varp, t + 1)
                    scalar.copy(out=ar[:, 121:122], in_=ar[:, 122:123])
                    scalar.copy(out=ar[:, 121:122], in_=ar[:, 122:123])
                    scalar.activation(out=varp, in_=mv[:, 1:2],
                                      func=AF.Copy, bias=EPS)
                    scalar.activation(out=lnv, in_=varp, func=AF.Ln)
                    scalar.activation(out=inv, in_=lnv, func=AF.Exp,
                                      scale=-0.5)
                    scalar.drain()
                    scalar.sem_inc(s_inv, 1)
                    for b in range(8):
                        scalar.wait_ge(s_tr, t * 8 + b + 1)
                        off = (t * ND + b * 4) * 128
                        ins = scalar.copy(
                            out=xn_fm[:, off:off + 512],
                            in_=ptr[b % 2][:, :],
                        )
                        if b == 7:
                            scalar.drain()
                            scalar.sem_inc(s_trfree, 1)
                        else:
                            ins.then_inc(s_trfree, 1)
                    if t >= 1:
                        scalar.wait_ge(s_ls, t)
                        scalar.copy(out=ar[:, 121:122], in_=ar[:, 122:123])
                        scalar.copy(out=ar[:, 121:122], in_=ar[:, 122:123])
                        scalar.activation(out=e_, in_=ls, func=AF.Exp)
                        scalar.drain()
                        scalar.sem_inc(s_e, 1)
                        scalar.wait_ge(s_hwtr, t)
                        scalar.copy(out=ar[:, 121:122], in_=ar[:, 122:123])
                        scalar.copy(out=ar[:, 121:122], in_=ar[:, 122:123])
                        scalar.copy(
                            out=hwfm[:, (t - 1) * 128:t * 128],
                            in_=phw[:, :],
                        )
                        scalar.drain()
                        scalar.sem_inc(s_hwfm, 1)
                if True:
                    scalar.wait_ge(s_ls, NT)
                    scalar.copy(out=ar[:, 121:122], in_=ar[:, 122:123])
                    scalar.copy(out=ar[:, 121:122], in_=ar[:, 122:123])
                    scalar.activation(out=e_, in_=ls, func=AF.Exp)
                    scalar.drain()
                    scalar.sem_inc(s_e, 1)
                    scalar.wait_ge(s_hwtr, NT)
                    scalar.copy(out=ar[:, 121:122], in_=ar[:, 122:123])
                    scalar.copy(out=ar[:, 121:122], in_=ar[:, 122:123])
                    scalar.copy(
                        out=hwfm[:, (NT - 1) * 128:NT * 128],
                        in_=phw[:, :],
                    )
                    scalar.drain()
                    scalar.sem_inc(s_hwfm, 1)

            @block.vector
            def _(v):
                for t in range(NT):
                    if t == 0:
                        v.wait_ge(s_wm_p[0], 32)
                        v.wait_ge(s_wm_p[1], 32)
                    la = t * NH + 2
                    if la < NT * NH:
                        v.wait_ge(s_fl3[la % 3], 16 * (la // 3 + 1))
                    for h in range(NH):
                        li = t * NH + h
                        b3 = li % 3
                        v.wait_ge(s_xh3[b3], 16 * (li // 3 + 1))
                        v.wait_ge(s_fl3[b3], 16 * (li // 3 + 1))
                        for j in range(4):
                            v.bn_stats(
                                out=stats[:, (h * 4 + j) * 6:(h * 4 + j + 1) * 6],
                                in_=xh[b3][:, j * 512:(j + 1) * 512])
                    v.bn_aggr(out=mv, in_=stats)
                    v.drain()
                    v.sem_inc(s_varp, 1)
                    v.wait_ge(s_inv, t + 1)
                    v.tensor_copy(out=ar[:, 124:126], in_=ar[:, 126:128])
                    v.tensor_copy(out=ar[:, 124:126], in_=ar[:, 126:128])
                    for q in range(4):
                        li = t * NH + q // 2
                        w = 8 * t + 2 * q
                        if w > 0:
                            v.wait_ge(s_tr, w)
                        v.tensor_scalar(
                            out=xnb[:, :],
                            in0=xh[li % 3][:, (q % 2) * 1024:(q % 2 + 1) * 1024],
                            scalar1=mv[:, 0:1], scalar2=inv,
                            op0=OP.subtract, op1=OP.mult,
                        )
                        v.drain()
                        v.sem_inc(s_xn, 1)
                    if t == 0:
                        continue
                    t = t - 1
                    v.wait_ge(s_gh, min(t + 2, NT + 1))
                    if t >= 1:
                        v.wait_ge(s_hwtr, t)
                    v.reduce_max(out=m1, in_=pgh[t % 2][:, 0:8], axis=AX.X)
                    v.tensor_copy(out=ar[:, 124:126], in_=ar[:, 126:128])
                    v.tensor_copy(out=ar[:, 124:126], in_=ar[:, 126:128])
                    v.tensor_scalar(out=ls, in0=pgh[t % 2][:, 0:8],
                                    scalar1=m1, scalar2=0.0,
                                    op0=OP.subtract, op1=OP.min)
                    v.tensor_copy(out=ar[:, 124:126], in_=ar[:, 126:128])
                    v.tensor_copy(out=ar[:, 124:126], in_=ar[:, 126:128])
                    v.drain()
                    v.sem_inc(s_ls, 1)
                    v.tensor_scalar(out=tmask, in0=ls, scalar1=0.0,
                                    scalar2=None, op0=OP.is_ge)
                    v.tensor_copy(out=ar[:, 124:126], in_=ar[:, 126:128])
                    v.tensor_copy(out=ar[:, 124:126], in_=ar[:, 126:128])
                    v.scalar_tensor_tensor(out=ls2, in0=tmask,
                                           scalar=-1e30, in1=ls,
                                           op0=OP.mult, op1=OP.add)
                    v.tensor_copy(out=ar[:, 124:126], in_=ar[:, 126:128])
                    v.tensor_copy(out=ar[:, 124:126], in_=ar[:, 126:128])
                    v.reduce_max(out=m2, in_=ls2, axis=AX.X)
                    v.tensor_copy(out=ar[:, 124:126], in_=ar[:, 126:128])
                    v.tensor_copy(out=ar[:, 124:126], in_=ar[:, 126:128])
                    v.tensor_scalar(out=keep, in0=ls, scalar1=m2,
                                    scalar2=None, op0=OP.is_ge)
                    v.tensor_copy(out=ar[:, 124:126], in_=ar[:, 126:128])
                    v.tensor_copy(out=ar[:, 124:126], in_=ar[:, 126:128])
                    v.wait_ge(s_e, t + 1)
                    v.tensor_copy(out=ar[:, 124:126], in_=ar[:, 126:128])
                    v.tensor_copy(out=ar[:, 124:126], in_=ar[:, 126:128])
                    v.tensor_tensor(out=ew, in0=e_, in1=keep, op=OP.mult)
                    v.tensor_copy(out=ar[:, 124:126], in_=ar[:, 126:128])
                    v.tensor_copy(out=ar[:, 124:126], in_=ar[:, 126:128])
                    v.tensor_reduce(out=ssum, in_=ew, axis=AX.X, op=OP.add)
                    v.tensor_copy(out=ar[:, 124:126], in_=ar[:, 126:128])
                    v.tensor_copy(out=ar[:, 124:126], in_=ar[:, 126:128])
                    v.tensor_scalar(out=ar[:, 30:31], in0=ssum, scalar1=1e-20,
                                    scalar2=None, op0=OP.max)
                    v.tensor_copy(out=ar[:, 124:126], in_=ar[:, 126:128])
                    v.tensor_copy(out=ar[:, 124:126], in_=ar[:, 126:128])
                    v.reciprocal(out=rinv, in_=ar[:, 30:31])
                    v.tensor_copy(out=ar[:, 124:126], in_=ar[:, 126:128])
                    v.tensor_copy(out=ar[:, 124:126], in_=ar[:, 126:128])
                    for e in range(8):
                        v.tensor_scalar(
                            out=hwb[:, e * 16:(e + 1) * 16],
                            in0=pgh[t % 2][:, 8 + e * 16:8 + (e + 1) * 16],
                            scalar1=ew[:, e:e + 1], scalar2=rinv,
                            op0=OP.mult, op1=OP.mult)
                    v.drain()
                    v.sem_inc(s_ghfree, 1)
                if True:
                    t = NT - 1
                    v.wait_ge(s_gh, NT + 1)
                    v.wait_ge(s_hwtr, t)
                    v.reduce_max(out=m1, in_=pgh[t % 2][:, 0:8], axis=AX.X)
                    v.tensor_copy(out=ar[:, 124:126], in_=ar[:, 126:128])
                    v.tensor_copy(out=ar[:, 124:126], in_=ar[:, 126:128])
                    v.tensor_scalar(out=ls, in0=pgh[t % 2][:, 0:8],
                                    scalar1=m1, scalar2=0.0,
                                    op0=OP.subtract, op1=OP.min)
                    v.tensor_copy(out=ar[:, 124:126], in_=ar[:, 126:128])
                    v.tensor_copy(out=ar[:, 124:126], in_=ar[:, 126:128])
                    v.drain()
                    v.sem_inc(s_ls, 1)
                    v.tensor_scalar(out=tmask, in0=ls, scalar1=0.0,
                                    scalar2=None, op0=OP.is_ge)
                    v.tensor_copy(out=ar[:, 124:126], in_=ar[:, 126:128])
                    v.tensor_copy(out=ar[:, 124:126], in_=ar[:, 126:128])
                    v.scalar_tensor_tensor(out=ls2, in0=tmask,
                                           scalar=-1e30, in1=ls,
                                           op0=OP.mult, op1=OP.add)
                    v.tensor_copy(out=ar[:, 124:126], in_=ar[:, 126:128])
                    v.tensor_copy(out=ar[:, 124:126], in_=ar[:, 126:128])
                    v.reduce_max(out=m2, in_=ls2, axis=AX.X)
                    v.tensor_copy(out=ar[:, 124:126], in_=ar[:, 126:128])
                    v.tensor_copy(out=ar[:, 124:126], in_=ar[:, 126:128])
                    v.tensor_scalar(out=keep, in0=ls, scalar1=m2,
                                    scalar2=None, op0=OP.is_ge)
                    v.tensor_copy(out=ar[:, 124:126], in_=ar[:, 126:128])
                    v.tensor_copy(out=ar[:, 124:126], in_=ar[:, 126:128])
                    v.wait_ge(s_e, t + 1)
                    v.tensor_copy(out=ar[:, 124:126], in_=ar[:, 126:128])
                    v.tensor_copy(out=ar[:, 124:126], in_=ar[:, 126:128])
                    v.tensor_tensor(out=ew, in0=e_, in1=keep, op=OP.mult)
                    v.tensor_copy(out=ar[:, 124:126], in_=ar[:, 126:128])
                    v.tensor_copy(out=ar[:, 124:126], in_=ar[:, 126:128])
                    v.tensor_reduce(out=ssum, in_=ew, axis=AX.X, op=OP.add)
                    v.tensor_copy(out=ar[:, 124:126], in_=ar[:, 126:128])
                    v.tensor_copy(out=ar[:, 124:126], in_=ar[:, 126:128])
                    v.tensor_scalar(out=ar[:, 30:31], in0=ssum, scalar1=1e-20,
                                    scalar2=None, op0=OP.max)
                    v.tensor_copy(out=ar[:, 124:126], in_=ar[:, 126:128])
                    v.tensor_copy(out=ar[:, 124:126], in_=ar[:, 126:128])
                    v.reciprocal(out=rinv, in_=ar[:, 30:31])
                    v.tensor_copy(out=ar[:, 124:126], in_=ar[:, 126:128])
                    v.tensor_copy(out=ar[:, 124:126], in_=ar[:, 126:128])
                    for e in range(8):
                        v.tensor_scalar(
                            out=hwb[:, e * 16:(e + 1) * 16],
                            in0=pgh[t % 2][:, 8 + e * 16:8 + (e + 1) * 16],
                            scalar1=ew[:, e:e + 1], scalar2=rinv,
                            op0=OP.mult, op1=OP.mult)
                    v.drain()
                    v.sem_inc(s_ghfree, 1)

                for gi in range(NG):
                    k, t = kt(gi)
                    v.wait_ge(s_mm, min(gi + 3, NG + 1))
                    if gi % NTG == 0:
                        v.wait_ge(s_bm_p[k % 2], 32 * (k // 2 + 1))
                    v.wait_ge(s_xr[gi % 3], 16 * (gi // 3 + 1))
                    if gi >= 2:
                        v.wait_ge(s_st_p[gi % 2], 16 * (gi // 2))
                    v.tensor_tensor(
                        out=outsb[gi % 2],
                        in0=pbig[gi % 4][:, :],
                        in1=bmsl[:, (k % 2) * OC:(k % 2 + 1) * OC],
                        op=OP.add,
                    )
                    v.tensor_tensor(
                        out=outsb[gi % 2],
                        in0=outsb[gi % 2],
                        in1=xres[gi % 3],
                        op=OP.add,
                    )
                    v.drain()
                    v.sem_inc(s_evac, 1)

            @block.tensor
            def _(te):
                te.wait_ge(s_ident, 16)
                te.wait_ge(s_fl3[0], 16)
                te.wait_ge(s_fl3[1], 16)
                for t in range(NT):
                    for b in range(8):
                        gb = t * 8 + b
                        te.wait_ge(s_xn, t * 4 + b // 2 + 1)
                        if gb >= 2:
                            te.wait_ge(s_trfree, gb - 1)
                        for j in range(4):
                            col = (b % 2) * 512 + j * 128
                            ins = te.transpose(
                                out=ptr[b % 2][:, j * 128:(j + 1) * 128],
                                in_=xnb[:, col:col + 128],
                                identity=ident[:, :])
                        ins.then_inc(s_tr, 1)
                    if t == 0:
                        te.wait_ge(s_const, 16)
                        te.wait_ge(s_wm_p[0], 32)
                    te.wait_ge(s_trfree, 8 * (t + 1))
                    if t > 1:
                        te.wait_ge(s_ghfree, t - 1)
                    for d in range(ND):
                        ins = te.matmul(
                            out=pgh[t % 2][:, :],
                            lhsT=fm(t, d),
                            rhs=wga_sb[:, d * GH:(d + 1) * GH],
                            start=(d == 0), stop=(d == ND - 1))
                    ins.then_inc(s_gh, 1)
                    if t > 0:
                        te.wait_ge(s_ghfree, t)
                        if t > 1:
                            te.wait_ge(s_hwfm, t - 1)
                        te.transpose(out=phw[:, :], in_=hwb[:, :],
                                     identity=ident[:, :])
                        te.matmul(out=pbig[0][:, 0:128], lhsT=ident[:, :],
                                  rhs=ident[:, :], start=True,
                                  stop=True).then_inc(s_hwtr, 1)
                te.matmul(out=pbig[0][:, 0:128], lhsT=ident[:, :],
                          rhs=ident[:, :], start=True,
                          stop=True).then_inc(s_gh, 1)
                te.wait_ge(s_ghfree, NT)
                te.wait_ge(s_hwfm, NT - 1)
                te.transpose(out=phw[:, :], in_=hwb[:, :],
                             identity=ident[:, :])
                te.matmul(out=pbig[0][:, 0:128], lhsT=ident[:, :],
                          rhs=ident[:, :], start=True,
                          stop=True).then_inc(s_hwtr, 1)

                for gi in range(NG):
                    k, t = kt(gi)
                    if gi % NTG == 0:
                        te.wait_ge(s_wm_p[k % 2], 32 * (k // 2 + 1))
                        te.wait_ge(s_bm_p[k % 2], 32 * (k // 2 + 1))

                    if gi >= 4:
                        te.wait_ge(s_evac, gi - 3)
                    for d in range(ND):
                        te.matmul(
                            out=pbig[gi % 4][:, :],
                            lhsT=fm(t, d),
                            rhs=wm[k % 2][:, d * OC:(d + 1) * OC],
                            start=(d == 0), stop=False)
                    if k == 0:
                        te.wait_ge(s_hwfm, t + 1)
                    te.matmul(
                        out=pbig[gi % 4][:, :],
                        lhsT=hwfm[:, t * 128:(t + 1) * 128],
                        rhs=bfl_sl[:, (k % 2) * OC:(k % 2 + 1) * OC],
                        start=False, stop=True).then_inc(s_mm, 1)
                    if gi % NTG == NTG - 1:
                        te.matmul(out=pgh[0][:, 0:128], lhsT=ident[:, :],
                                  rhs=ident[:, :], start=True,
                                  stop=True).then_inc(s_wmfree, 1)
                te.transpose(out=ptr[0][:, 0:128], in_=ident[:, :],
                             identity=ident[:, :])
                te.transpose(out=ptr[0][:, 128:256], in_=ident[:, :],
                             identity=ident[:, :]).then_inc(s_mm, 1)

    return nc


def kernel(L0_attn_x, L0_attn_weight, L0_attn_bias, L0_ffn_W_base,
           L0_ffn_b_base, L0_ffn_W_gate, L0_ffn_A, L0_ffn_B):
    x_full = np.ascontiguousarray(
        np.asarray(L0_attn_x, dtype=np.float32).reshape(B * S, D))

    def panels(W):
        return np.ascontiguousarray(
            np.asarray(W, np.float32).reshape(NO, OC, ND, 128)
            .transpose(0, 3, 2, 1).reshape(NO * 128, ND * OC))

    wtp = panels(L0_attn_weight)
    wbtp = panels(L0_ffn_W_base)
    wg = np.asarray(L0_ffn_W_gate, np.float32)
    a = np.asarray(L0_ffn_A, np.float32).reshape(128, D)
    wga = np.ascontiguousarray(
        np.concatenate([wg.T, a.T], axis=1))
    bfl = np.ascontiguousarray(
        np.asarray(L0_ffn_B, np.float32).transpose(0, 2, 1).reshape(128, D))
    ba = np.ascontiguousarray(np.asarray(L0_attn_bias, np.float32))
    bb = np.ascontiguousarray(np.asarray(L0_ffn_b_base, np.float32))
    idn = np.eye(128, dtype=ml_dtypes.bfloat16)

    nc = build_nc()
    in_maps = []
    for c in range(N_CORES):
        in_maps.append({
            "x": np.ascontiguousarray(x_full[c * TOK:(c + 1) * TOK]),
            "wtp": wtp, "wbtp": wbtp, "wga": wga, "bfl": bfl,
            "ba": ba, "bb": bb, "idn": idn,
        })
    res = run_bass_kernel_spmd(nc, in_maps, core_ids=list(range(N_CORES)))
    shards = [np.asarray(r["out"], dtype=np.float32) for r in res.results]
    return np.concatenate(shards, axis=0).reshape(B, S, D)


# version 09 (final)
# speedup vs baseline: 1.2835x; 1.0906x over previous
import numpy as np
import ml_dtypes

import concourse.bass as bass
import concourse.mybir as mybir
from concourse.bass_utils import run_bass_kernel_spmd

F32 = mybir.dt.float32
BF16 = mybir.dt.bfloat16
AF = mybir.ActivationFunctionType
OP = mybir.AluOpType
AX = mybir.AxisListType

N_CORES = 8
B, S, D = 4, 4096, 4096
TOK = (B * S) // N_CORES
NT = TOK // 128
ND = D // 128
OC = 256
NO = D // OC
NG = NO * NT
NH = 2
HW = D // NH
GH = 136
EPS = 1e-5


def build_nc():
    nc = bass.Bass("TRN2", target_bir_lowering=False,
                   dynamic_dma_scratch_size=14336)

    x = nc.dram_tensor("x", [TOK, D], F32, kind="ExternalInput")
    wtp = nc.dram_tensor("wtp", [NO * 128, ND * OC], F32, kind="ExternalInput")
    wbtp = nc.dram_tensor("wbtp", [NO * 128, ND * OC], F32, kind="ExternalInput")
    wgap = nc.dram_tensor("wgap", [128, ND * GH], F32, kind="ExternalInput")
    bfl = nc.dram_tensor("bfl", [128, D], F32, kind="ExternalInput")
    ba = nc.dram_tensor("ba", [D], F32, kind="ExternalInput")
    bb = nc.dram_tensor("bb", [D], F32, kind="ExternalInput")
    idn = nc.dram_tensor("idn", [128, 128], BF16, kind="ExternalInput")
    out = nc.dram_tensor("out", [TOK, D], F32, kind="ExternalOutput")
    bmg = nc.dram_tensor("bmg", [D], F32)
    bmgb = nc.dram_tensor("bmgb", [D], BF16)
    bmgB = nc.dram_tensor("bmgB", [128, D], BF16)
    bflg = nc.dram_tensor("bflg", [128, D], BF16)

    import contextlib
    with contextlib.ExitStack() as st:
        def sb(name, shape, dt):
            return st.enter_context(nc.sbuf_tensor(name, shape, dt))

        def ps(name, shape, dt):
            return st.enter_context(nc.psum_tensor(name, shape, dt))

        def sem(name):
            return st.enter_context(nc.semaphore(name))

        guard = sb("guard", [128, 1024], BF16)
        xn_fm = sb("xn_fm", [128, NT * ND * 128], BF16)
        wm = [sb("wm0", [128, ND * OC], BF16),
              sb("wm1", [128, ND * OC], BF16)]
        xh_a = sb("xh_a", [128, 3 * HW], F32)
        xh = [xh_a[:, i * HW:(i + 1) * HW] for i in range(3)]
        xnb = sb("xnb", [128, HW // 2], BF16)
        wga_sb = sb("wga_sb", [128, ND * GH], BF16)
        hwfm = sb("hwfm", [128, TOK], BF16)
        bfl_sl = sb("bfl_sl", [128, 2 * OC], BF16)
        outsb_a = sb("outsb_a", [128, 2 * OC], F32)
        outsb = [outsb_a[:, i * OC:(i + 1) * OC] for i in range(2)]
        xres_a = sb("xres_a", [128, 3 * OC], F32)
        xres = [xres_a[:, i * OC:(i + 1) * OC] for i in range(3)]
        bmsl = sb("bmsl", [128, 2 * OC], BF16)
        ident = sb("ident", [128, 128], BF16)
        ar = sb("arena", [128, 128], F32)
        stats = ar[:, 0:48]
        mv = ar[:, 48:50]
        varp = ar[:, 50:51]
        lnv = ar[:, 51:52]
        inv = ar[:, 52:53]
        ls = ar[:, 56:64]
        e_ = ar[:, 64:72]
        tmask = ar[:, 72:80]
        ls2 = ar[:, 80:88]
        keep = ar[:, 88:96]
        ew = ar[:, 96:104]
        m1 = ar[:, 104:105]
        m2 = ar[:, 105:106]
        ssum = ar[:, 106:107]
        rinv = ar[:, 107:108]
        dumf = ar[:, 108:112]
        hwb = sb("hwb", [128, 128], BF16)

        pbig = [ps(f"pbig{i}", [128, OC], F32) for i in range(4)]
        pgh = [ps("pgh0", [128, GH], F32), ps("pgh1", [128, GH], F32)]
        ptr0w = ps("ptr0", [128, 768], BF16)
        ptr = [ptr0w[:, 0:512], ps("ptr1", [128, 512], BF16)]
        phw = ptr0w[:, 512:640]

        s_const = sem("s_const")
        s_ident = sem("s_ident")
        s_xh3 = [sem(f"s_xh{i}") for i in range(3)]
        s_fl3 = [sem(f"s_fl{i}") for i in range(3)]
        s_xn = sem("s_xn")
        s_varp = sem("s_varp")
        s_inv = sem("s_inv")
        s_tr = sem("s_tr")
        s_trfree = sem("s_trfree")
        s_gh = sem("s_gh")
        s_ls = sem("s_ls")
        s_e = sem("s_e")
        s_ghfree = sem("s_ghfree")
        s_hwtr = sem("s_hwtr")
        s_hwfm = sem("s_hwfm")
        s_wm_p = [sem("s_wm0"), sem("s_wm1")]
        s_wmfree = sem("s_wmfree")
        s_bm_p = [sem("s_bm0"), sem("s_bm1")]
        s_xr = [sem(f"s_xr{i}") for i in range(3)]
        s_mm = sem("s_mm")
        s_evac = sem("s_evac")
        s_st_p = [sem("s_st0"), sem("s_st1")]
        s_wfin = sem("s_wfin")
        s_bmg = sem("s_bmg")



        def fm(t, d):
            off = (t * ND + d) * 128
            return xn_fm[:, off:off + 128]

        NTG = NT

        def kt(gi):
            return gi // NTG, gi % NTG

        with nc.Block() as block:

            @block.sync
            def _(sync):
                sync.dma_start(out=ident[:, :], in_=idn[:, :]).then_inc(s_ident, 16)
                for t in range(NT):
                    for h in range(NH):
                        li = t * NH + h
                        b3 = li % 3
                        if li >= 3:
                            sync.wait_ge(s_xn, 2 * li - 4)
                        sync.dma_start(
                            out=xh[b3],
                            in_=x[t * 128:(t + 1) * 128, h * HW:(h + 1) * HW],
                        ).then_inc(s_xh3[b3], 16)
                        sync.dma_start(
                            out=dumf,
                            in_=xh[b3][:, HW - 4:HW],
                        ).then_inc(s_fl3[b3], 16)

                def sync_fill_bm(k):
                    sync.dma_start(
                        out=bmsl[:, (k % 2) * OC:(k % 2 + 1) * OC],
                        in_=bmgB[:, k * OC:(k + 1) * OC],
                    ).then_inc(s_bm_p[k % 2], 16)
                    sync.dma_start(
                        out=bfl_sl[:, (k % 2) * OC:(k % 2 + 1) * OC],
                        in_=bflg[:, k * OC:(k + 1) * OC],
                    ).then_inc(s_bm_p[k % 2], 16)

                sync.wait_ge(s_bmg, 80)
                sync_fill_bm(0)
                sync_fill_bm(1)
                for gi in range(3):
                    k, t = kt(gi)
                    sync.dma_start(
                        out=xres[gi % 3],
                        in_=x[t * 128:(t + 1) * 128, k * OC:(k + 1) * OC],
                    ).then_inc(s_xr[gi % 3], 16)
                for gi in range(NG):
                    k, t = kt(gi)
                    sync.wait_ge(s_evac, gi + 1)
                    sync.dma_start(
                        out=out[t * 128:(t + 1) * 128, k * OC:(k + 1) * OC],
                        in_=outsb[gi % 2],
                    ).then_inc(s_st_p[gi % 2], 16)
                    if gi % NTG == NTG - 1 and k + 2 < NO:
                        sync_fill_bm(k + 2)
                    if gi + 3 < NG:
                        k2, t2 = kt(gi + 3)
                        sync.dma_start(
                            out=xres[gi % 3],
                            in_=x[t2 * 128:(t2 + 1) * 128,
                                  k2 * OC:(k2 + 1) * OC],
                        ).then_inc(s_xr[gi % 3], 16)
                sync.wait_ge(s_st_p[0], 16 * NG // 2)
                sync.wait_ge(s_st_p[1], 16 * NG // 2)


            @block.gpsimd
            def _(g):
                def fill_wm(k):
                    par = k % 2
                    base = 32 * (k // 2)
                    g.dma_start(out=wm[par][:, :],
                                in_=wtp[k * 128:(k + 1) * 128, :],
                                max_dma_last_dim=8192
                                ).then_inc(s_wm_p[par], 16)
                    g.wait_ge(s_wm_p[par], base + 16)
                    g.dma_start(out=wm[par][:, :],
                                in_=wbtp[k * 128:(k + 1) * 128, :],
                                accum_op=OP.add,
                                max_dma_last_dim=8192
                                ).then_inc(s_wm_p[par], 16)
                    g.wait_ge(s_wm_p[par], base + 32)

                g.dma_start(out=wga_sb[:, :], in_=wgap[:, :],
                            max_dma_last_dim=8192).then_inc(s_const, 16)
                fill_wm(0)
                fill_wm(1)
                g.dma_start(out=bmg[:], in_=ba[:]).then_inc(s_bmg, 16)
                g.wait_ge(s_bmg, 16)
                g.dma_start(out=bmg[:], in_=bb[:],
                            accum_op=OP.add).then_inc(s_bmg, 16)
                g.wait_ge(s_bmg, 32)
                g.dma_start(out=bmgb[:], in_=bmg[:],
                            max_dma_last_dim=8192).then_inc(s_bmg, 16)
                g.wait_ge(s_bmg, 48)
                g.dma_start(out=bmgB[:, :],
                            in_=bass.AP(bmgb, 0, [[0, 128], [1, D]]),
                            max_dma_last_dim=4096).then_inc(s_bmg, 16)
                g.dma_start(out=bflg[:, :], in_=bfl[:, :],
                            max_dma_last_dim=8192).then_inc(s_bmg, 16)
                for k in range(NO - 2):
                    g.wait_ge(s_wmfree, k + 1)
                    fill_wm(k + 2)


            @block.scalar
            def _(scalar):
                for t in range(NT):
                    scalar.wait_ge(s_varp, t + 1)
                    scalar.copy(out=ar[:, 121:122], in_=ar[:, 122:123])
                    scalar.copy(out=ar[:, 121:122], in_=ar[:, 122:123])
                    scalar.activation(out=varp, in_=mv[:, 1:2],
                                      func=AF.Copy, bias=EPS)
                    scalar.activation(out=lnv, in_=varp, func=AF.Ln)
                    scalar.activation(out=inv, in_=lnv, func=AF.Exp,
                                      scale=-0.5)
                    scalar.drain()
                    scalar.sem_inc(s_inv, 1)
                    if t >= 1:
                        scalar.wait_ge(s_ls, t)
                        scalar.copy(out=ar[:, 121:122], in_=ar[:, 122:123])
                        scalar.copy(out=ar[:, 121:122], in_=ar[:, 122:123])
                        scalar.activation(out=e_, in_=ls, func=AF.Exp)
                        scalar.drain()
                        scalar.sem_inc(s_e, 1)
                    for b in range(8):
                        scalar.wait_ge(s_tr, t * 8 + b + 1)
                        off = (t * ND + b * 4) * 128
                        ins = scalar.copy(
                            out=xn_fm[:, off:off + 512],
                            in_=ptr[b % 2][:, :],
                        )
                        if b == 7:
                            scalar.drain()
                            scalar.sem_inc(s_trfree, 1)
                        else:
                            ins.then_inc(s_trfree, 1)
                    if t >= 1:
                        scalar.wait_ge(s_hwtr, t)
                        scalar.copy(out=ar[:, 121:122], in_=ar[:, 122:123])
                        scalar.copy(out=ar[:, 121:122], in_=ar[:, 122:123])
                        scalar.copy(
                            out=hwfm[:, (t - 1) * 128:t * 128],
                            in_=phw[:, :],
                        )
                        scalar.drain()
                        scalar.sem_inc(s_hwfm, 1)
                if True:
                    scalar.wait_ge(s_ls, NT)
                    scalar.copy(out=ar[:, 121:122], in_=ar[:, 122:123])
                    scalar.copy(out=ar[:, 121:122], in_=ar[:, 122:123])
                    scalar.activation(out=e_, in_=ls, func=AF.Exp)
                    scalar.drain()
                    scalar.sem_inc(s_e, 1)
                    scalar.wait_ge(s_hwtr, NT)
                    scalar.copy(out=ar[:, 121:122], in_=ar[:, 122:123])
                    scalar.copy(out=ar[:, 121:122], in_=ar[:, 122:123])
                    scalar.copy(
                        out=hwfm[:, (NT - 1) * 128:NT * 128],
                        in_=phw[:, :],
                    )
                    scalar.drain()
                    scalar.sem_inc(s_hwfm, 1)

            @block.vector
            def _(v):
                def dum():
                    v.tensor_copy(out=ar[:, 124:126], in_=ar[:, 126:128])
                    v.tensor_copy(out=ar[:, 124:126], in_=ar[:, 126:128])

                def epi_a(t):
                    v.wait_ge(s_gh, t + 1)
                    if t >= 1:
                        v.wait_ge(s_hwtr, t)
                    v.reduce_max(out=m1, in_=pgh[t % 2][:, 0:8], axis=AX.X)
                    dum()
                    v.tensor_scalar(out=ls, in0=pgh[t % 2][:, 0:8],
                                    scalar1=m1, scalar2=0.0,
                                    op0=OP.subtract, op1=OP.min)
                    dum()
                    v.drain()
                    v.sem_inc(s_ls, 1)

                def epi_b(t):
                    v.tensor_scalar(out=tmask, in0=ls, scalar1=0.0,
                                    scalar2=None, op0=OP.is_ge)
                    dum()
                    v.scalar_tensor_tensor(out=ls2, in0=tmask,
                                           scalar=-1e30, in1=ls,
                                           op0=OP.mult, op1=OP.add)
                    dum()
                    v.reduce_max(out=m2, in_=ls2, axis=AX.X)
                    dum()
                    v.tensor_scalar(out=keep, in0=ls, scalar1=m2,
                                    scalar2=None, op0=OP.is_ge)
                    dum()
                    v.wait_ge(s_e, t + 1)
                    dum()
                    v.tensor_tensor(out=ew, in0=e_, in1=keep, op=OP.mult)
                    dum()
                    v.tensor_reduce(out=ssum, in_=ew, axis=AX.X, op=OP.add)
                    dum()
                    v.tensor_scalar(out=ar[:, 30:31], in0=ssum, scalar1=1e-20,
                                    scalar2=None, op0=OP.max)
                    dum()
                    v.reciprocal(out=rinv, in_=ar[:, 30:31])
                    dum()
                    for e in range(8):
                        v.tensor_scalar(
                            out=hwb[:, e * 16:(e + 1) * 16],
                            in0=pgh[t % 2][:, 8 + e * 16:8 + (e + 1) * 16],
                            scalar1=ew[:, e:e + 1], scalar2=rinv,
                            op0=OP.mult, op1=OP.mult)
                    v.drain()
                    v.sem_inc(s_ghfree, 1)

                for t in range(NT):
                    if t == 0:
                        v.wait_ge(s_wm_p[0], 16)
                    la = t * NH + 2
                    if la < NT * NH:
                        v.wait_ge(s_fl3[la % 3], 16 * (la // 3 + 1))
                    for h in range(NH):
                        li = t * NH + h
                        b3 = li % 3
                        v.wait_ge(s_xh3[b3], 16 * (li // 3 + 1))
                        v.wait_ge(s_fl3[b3], 16 * (li // 3 + 1))
                        for j in range(4):
                            v.bn_stats(
                                out=stats[:, (h * 4 + j) * 6:(h * 4 + j + 1) * 6],
                                in_=xh[b3][:, j * 512:(j + 1) * 512])
                    v.bn_aggr(out=mv, in_=stats)
                    v.drain()
                    v.sem_inc(s_varp, 1)
                    if t >= 1:
                        epi_a(t - 1)
                    v.wait_ge(s_inv, t + 1)
                    dum()
                    for q in range(4):
                        li = t * NH + q // 2
                        w = 8 * t + 2 * q
                        if w > 0:
                            v.wait_ge(s_tr, w)
                        v.tensor_scalar(
                            out=xnb[:, :],
                            in0=xh[li % 3][:, (q % 2) * 1024:(q % 2 + 1) * 1024],
                            scalar1=mv[:, 0:1], scalar2=inv,
                            op0=OP.subtract, op1=OP.mult,
                        )
                        v.drain()
                        v.sem_inc(s_xn, 1)
                    if t >= 1:
                        epi_b(t - 1)
                epi_a(NT - 1)
                epi_b(NT - 1)

                for gi in range(NG):
                    k, t = kt(gi)
                    v.wait_ge(s_mm, min(gi + 2, NG + 1))
                    if gi % NTG == 0:
                        v.wait_ge(s_bm_p[k % 2], 32 * (k // 2 + 1))
                    v.wait_ge(s_xr[gi % 3], 16 * (gi // 3 + 1))
                    if gi >= 2:
                        v.wait_ge(s_st_p[gi % 2], 16 * (gi // 2))
                    v.tensor_tensor(
                        out=outsb[gi % 2],
                        in0=pbig[gi % 4][:, :],
                        in1=bmsl[:, (k % 2) * OC:(k % 2 + 1) * OC],
                        op=OP.add,
                    )
                    v.tensor_tensor(
                        out=outsb[gi % 2],
                        in0=outsb[gi % 2],
                        in1=xres[gi % 3],
                        op=OP.add,
                    )
                    v.drain()
                    v.sem_inc(s_evac, 1)

            @block.tensor
            def _(te):
                te.wait_ge(s_ident, 16)
                te.wait_ge(s_fl3[0], 16)
                te.wait_ge(s_fl3[1], 16)
                for t in range(NT):
                    for b in range(8):
                        gb = t * 8 + b
                        te.wait_ge(s_xn, t * 4 + b // 2 + 1)
                        if gb >= 2:
                            te.wait_ge(s_trfree, gb - 1)
                        for j in range(4):
                            col = (b % 2) * 512 + j * 128
                            ins = te.transpose(
                                out=ptr[b % 2][:, j * 128:(j + 1) * 128],
                                in_=xnb[:, col:col + 128],
                                identity=ident[:, :])
                        ins.then_inc(s_tr, 1)
                    if t == 0:
                        te.wait_ge(s_const, 16)
                        te.wait_ge(s_wm_p[0], 32)
                    te.wait_ge(s_trfree, 8 * (t + 1))
                    if t > 1:
                        te.wait_ge(s_ghfree, t - 1)
                    for d in range(ND):
                        te.matmul(
                            out=pgh[t % 2][:, :],
                            lhsT=fm(t, d),
                            rhs=wga_sb[:, d * GH:(d + 1) * GH],
                            start=(d == 0), stop=(d == ND - 1))
                    te.matmul(out=pbig[0][:, 0:128], lhsT=ident[:, :],
                              rhs=ident[:, :], start=True, stop=True)
                    te.matmul(out=pbig[0][:, 0:128], lhsT=ident[:, :],
                              rhs=ident[:, :], start=True,
                              stop=True).then_inc(s_gh, 1)
                    if t > 0:
                        te.wait_ge(s_ghfree, t)
                        if t > 1:
                            te.wait_ge(s_hwfm, t - 1)
                        te.transpose(out=phw[:, :], in_=hwb[:, :],
                                     identity=ident[:, :])
                        te.matmul(out=pbig[0][:, 0:128], lhsT=ident[:, :],
                                  rhs=ident[:, :], start=True,
                                  stop=True).then_inc(s_hwtr, 1)
                te.wait_ge(s_ghfree, NT)
                te.wait_ge(s_hwfm, NT - 1)
                te.transpose(out=phw[:, :], in_=hwb[:, :],
                             identity=ident[:, :])
                te.matmul(out=pbig[0][:, 0:128], lhsT=ident[:, :],
                          rhs=ident[:, :], start=True,
                          stop=True).then_inc(s_hwtr, 1)

                for gi in range(NG):
                    k, t = kt(gi)
                    if gi % NTG == 0:
                        te.wait_ge(s_wm_p[k % 2], 32 * (k // 2 + 1))
                        te.wait_ge(s_bm_p[k % 2], 32 * (k // 2 + 1))

                    if gi >= 4:
                        te.wait_ge(s_evac, gi - 3)
                    for d in range(ND):
                        te.matmul(
                            out=pbig[gi % 4][:, :],
                            lhsT=fm(t, d),
                            rhs=wm[k % 2][:, d * OC:(d + 1) * OC],
                            start=(d == 0), stop=False)
                    if k == 0:
                        te.wait_ge(s_hwfm, t + 1)
                    te.matmul(
                        out=pbig[gi % 4][:, :],
                        lhsT=hwfm[:, t * 128:(t + 1) * 128],
                        rhs=bfl_sl[:, (k % 2) * OC:(k % 2 + 1) * OC],
                        start=False, stop=True).then_inc(s_mm, 1)
                    if gi % NTG == NTG - 1:
                        te.matmul(out=pgh[0][:, 0:128], lhsT=ident[:, :],
                                  rhs=ident[:, :], start=True,
                                  stop=True).then_inc(s_wmfree, 1)
                te.transpose(out=ptr[0][:, 0:128], in_=ident[:, :],
                             identity=ident[:, :])
                te.transpose(out=ptr[0][:, 128:256], in_=ident[:, :],
                             identity=ident[:, :]).then_inc(s_mm, 1)

    return nc


def kernel(L0_attn_x, L0_attn_weight, L0_attn_bias, L0_ffn_W_base,
           L0_ffn_b_base, L0_ffn_W_gate, L0_ffn_A, L0_ffn_B):
    x_full = np.ascontiguousarray(
        np.asarray(L0_attn_x, dtype=np.float32).reshape(B * S, D))

    def panels(W):
        return np.ascontiguousarray(
            np.asarray(W, np.float32).reshape(NO, OC, ND, 128)
            .transpose(0, 3, 2, 1).reshape(NO * 128, ND * OC))

    wtp = panels(L0_attn_weight)
    wbtp = panels(L0_ffn_W_base)
    wg = np.asarray(L0_ffn_W_gate, np.float32)
    a = np.asarray(L0_ffn_A, np.float32).reshape(128, D)
    wga = np.concatenate([wg.T, a.T], axis=1)
    wgap = np.ascontiguousarray(
        wga.reshape(ND, 128, GH).transpose(1, 0, 2).reshape(128, ND * GH))
    bfl = np.ascontiguousarray(
        np.asarray(L0_ffn_B, np.float32).transpose(0, 2, 1).reshape(128, D))
    ba = np.ascontiguousarray(np.asarray(L0_attn_bias, np.float32))
    bb = np.ascontiguousarray(np.asarray(L0_ffn_b_base, np.float32))
    idn = np.eye(128, dtype=ml_dtypes.bfloat16)

    nc = build_nc()
    in_maps = []
    for c in range(N_CORES):
        in_maps.append({
            "x": np.ascontiguousarray(x_full[c * TOK:(c + 1) * TOK]),
            "wtp": wtp, "wbtp": wbtp, "wgap": wgap, "bfl": bfl,
            "ba": ba, "bb": bb, "idn": idn,
        })
    res = run_bass_kernel_spmd(nc, in_maps, core_ids=list(range(N_CORES)))
    shards = [np.asarray(r["out"], dtype=np.float32) for r in res.results]
    return np.concatenate(shards, axis=0).reshape(B, S, D)
